# version 07
# baseline (speedup 1.0000x reference)
import numpy as np
from contextlib import ExitStack

import concourse.bass as bass
import concourse.mybir as mybir
from concourse.bass_utils import run_bass_kernel_spmd

F32 = mybir.dt.float32
BF16 = mybir.dt.bfloat16
I32 = mybir.dt.int32
AF = mybir.ActivationFunctionType
ALU = mybir.AluOpType
AX = mybir.AxisListType

T = 2048
D = 4096
DFF = 11008
NH = 32
NT = T // 128
NDC = D // 128
NFC = DFF // 128
NQ = T // 512
C_SCALE = float(1.0 / np.sqrt(128.0))
EXP_BIAS = -12.0
LN_EPS = 1e-5
INV_D = 1.0 / D
PARTS = [(0, 29), (29, 29), (58, 28)]


def seg_list(qi):
    kend = (qi + 1) * 128
    if kend <= 1024:
        return [(0, kend)]
    return [(0, 1024), (1024, kend - 1024)]


def n_tgroups(qi):
    return -(-(qi + 1) // 4)


SEGS = []
SI_OF = {}
LAST_SI = {}
for _qi in range(NT):
    _sl = seg_list(_qi)
    for _j, (_s, _l) in enumerate(_sl):
        SI_OF[(_qi, _j)] = len(SEGS)
        SEGS.append((_qi, _j, _s, _l))
        LAST_SI[_qi] = len(SEGS) - 1
NSEG = len(SEGS)
TGC = [0]
for _qi in range(NT):
    TGC.append(TGC[-1] + n_tgroups(_qi))
NTG = TGC[-1]
NEV = NTG // 2
CMB_CUM = [0]
for _qi in range(NT):
    CMB_CUM.append(CMB_CUM[-1] + (1 if _qi < 8 else 2))


def dev_F(p, dm):
    if p == 0:
        return dm + 1
    if p == 1:
        return 32 + dm + 1
    return 64 + 2 * (dm + 1)


class Sem:
    def __init__(self, nc, e, name):
        self.h = e(nc.semaphore(name))


def w16(eng, sems, k):
    eng.wait_ge(sems[k % 2].h, 16 * (k // 2 + 1))


def wall16(eng, sems, n):
    eng.wait_ge(sems[0].h, 16 * ((n + 1) // 2))
    if n // 2:
        eng.wait_ge(sems[1].h, 16 * (n // 2))


def i16(ins, sems, k):
    ins.then_inc(sems[k % 2].h, 16)


def build_nc():
    nc = bass.Bass("TRN2", target_bir_lowering=False)

    x_d = nc.dram_tensor("x", [T, D], F32, kind="ExternalInput").ap()
    wq_d = nc.dram_tensor("wq_p", [NH, 128, D], F32, kind="ExternalInput").ap()
    wk_d = nc.dram_tensor("wk_p", [128, D], F32, kind="ExternalInput").ap()
    wv_d = nc.dram_tensor("wv_p", [128, D], F32, kind="ExternalInput").ap()
    wo_d = nc.dram_tensor("wo_p", [NDC, 128, D], F32, kind="ExternalInput").ap()
    gw_d = nc.dram_tensor("gw_p", [NFC, 128, D], I32, kind="ExternalInput").ap()
    uw_d = nc.dram_tensor("uw_p", [NFC, 128, D], I32, kind="ExternalInput").ap()
    dw_d = nc.dram_tensor("dw_p", [NDC, 128, DFF], I32, kind="ExternalInput").ap()
    gs_d = nc.dram_tensor("gs_p", [128, NFC], F32, kind="ExternalInput").ap()
    us_d = nc.dram_tensor("us_p", [128, NFC], F32, kind="ExternalInput").ap()
    ds_d = nc.dram_tensor("ds_p", [128, NDC], F32, kind="ExternalInput").ap()
    out_d = nc.dram_tensor("out", [T, D], F32, kind="ExternalOutput").ap()

    xn_dram = nc.dram_tensor("xn_dram", [T, D], BF16, kind="Internal").ap()
    oT_dram = nc.dram_tensor("oT_dram", [NH, 128, T], BF16, kind="Internal").ap()
    h_dram = nc.dram_tensor("h_dram", [DFF, T], BF16, kind="Internal").ap()
    attnT_dram = nc.dram_tensor("attnT_dram", [D, T], BF16, kind="Internal").ap()
    downT_dram = nc.dram_tensor("downT_dram", [D, T], BF16, kind="Internal").ap()
    dpart_dram = nc.dram_tensor("dpart_dram", [D, T], F32, kind="Internal").ap()

    with ExitStack() as ctx:
        e = ctx.enter_context

        cst = e(nc.sbuf_tensor("cst", [128, 8], F32))
        maskM = e(nc.sbuf_tensor("maskM", [128, 896], F32))
        ident = e(nc.sbuf_tensor("ident", [128, 128], BF16))
        ones_c = e(nc.sbuf_tensor("ones_c", [128, 1], BF16))
        ones_r = e(nc.sbuf_tensor("ones_r", [1, 128], F32))
        inv_sb = e(nc.sbuf_tensor("inv_sb", [1, 512], F32))
        invb = e(nc.sbuf_tensor("invb", [128, 512], BF16))
        gs_s = e(nc.sbuf_tensor("gs_s", [128, NFC], F32))
        us_s = e(nc.sbuf_tensor("us_s", [128, NFC], F32))
        ds_s = e(nc.sbuf_tensor("ds_s", [128, NDC], F32))
        kT = e(nc.sbuf_tensor("kT", [128, T], BF16))
        vtok = e(nc.sbuf_tensor("vtok", [128, T], BF16))
        sums0 = e(nc.sbuf_tensor("sums0", [128, 64], F32))
        sums1 = e(nc.sbuf_tensor("sums1", [128, 64], F32))
        stat = e(nc.sbuf_tensor("stat", [128, 16], F32))

        S = lambda name: Sem(nc, e, name)
        s_dsc = S("s_dsc")
        s_dxl = [S("s_dxl0"), S("s_dxl1")]
        s_dxn = [S("s_dxn0"), S("s_dxn1")]
        s_dxt = S("s_dxt")
        s_dwkv = S("s_dwkv")
        s_dwq = [S("s_dwq0"), S("s_dwq1")]
        s_dgw = [S("s_dgw0"), S("s_dgw1")]
        s_duw = [S("s_duw0"), S("s_duw1")]
        s_dwo = [S("s_dwo0"), S("s_dwo1")]
        s_ddw = [S("s_ddw0"), S("s_ddw1")]
        s_doS = [S("s_doS0"), S("s_doS1")]
        s_dolS = S("s_dolS")
        s_dolC = S("s_dolC")
        s_dhS = [S("s_dhS0"), S("s_dhS1")]
        s_dhL = S("s_dhL")
        s_daS = [S("s_daS0"), S("s_daS1")]
        s_dpS = [S("s_dpS0"), S("s_dpS1")]
        s_dpL = [S("s_dpL0"), S("s_dpL1")]
        s_dnS = [S("s_dnS0"), S("s_dnS1")]
        s_d6 = [S("s_d60"), S("s_d61")]
        s_dx6 = [S("s_dx60"), S("s_dx61")]
        s_dout = [S("s_dout0"), S("s_dout1")]
        s_pool = S("s_pool")
        s_lnV = S("s_lnV")
        s_lnA = S("s_lnA")
        s_kv = S("s_kv")
        s_kve = S("s_kve")
        s_vtr = S("s_vtr")
        s_vte = S("s_vte")
        s_q = S("s_q")
        s_qe = S("s_qe")
        s_st = S("s_st")
        s_exp = S("s_exp")
        s_msk = S("s_msk")
        s_on = S("s_on")
        s_iv = S("s_iv")
        s_ib = S("s_ib")
        s_ive = S("s_ive")
        s_pv = S("s_pv")
        s_pso = S("s_pso")
        s_gmm = S("s_gmm")
        s_umm = S("s_umm")
        s_silu = S("s_silu")
        s_hm = S("s_hm")
        s_omm = S("s_omm")
        s_oev = S("s_oev")
        s_dmm = S("s_dmm")
        s_dev = S("s_dev")
        s_sum = S("s_sum")
        s_dv2 = S("s_dv2")

        block = e(nc.Block())

        consts = [EXP_BIAS, C_SCALE, 0.0, 1.0, LN_EPS, -0.5]
        for i, cv in enumerate(consts):
            nc.const_aps.register(F32, cv, cst[:, i:i + 1])

        p1_cms = [nc.sbuf_tensor("xt0", [128, D], F32, side="right"),
                  nc.sbuf_tensor("xt1", [128, D], F32, side="right"),
                  nc.sbuf_tensor("xnb0", [128, D], BF16, side="right"),
                  nc.sbuf_tensor("xnb1", [128, D], BF16, side="right"),
                  nc.sbuf_tensor("sqscr", [128, D], BF16, side="right")]
        xt0, xt1, xnb0, xnb1, sqscr = [cm.__enter__() for cm in p1_cms]
        xt = [xt0, xt1]
        xnb = [xnb0, xnb1]

        @block.gpsimd
        def _(g):
            for i, cv in enumerate(consts):
                g.memset(cst[:, i:i + 1], cv)
            g.memset(maskM[:], 0.0).then_inc(s_pool.h, 1)
            g.memset(ident[:], 1.0).then_inc(s_pool.h, 1)
            g.memset(ones_c[:], 1.0)
            g.memset(ones_r[:], 1.0)
            g.wait_ge(s_pool.h, 2)
            g.affine_select(out=maskM[:], in_=maskM[:], pattern=[[1, 896]],
                            compare_op=ALU.is_ge, fill=-1e30, base=-384,
                            channel_multiplier=-1).then_inc(s_pool.h, 1)
            g.wait_ge(s_pool.h, 3)
            g.affine_select(out=ident[:], in_=ident[:], pattern=[[1, 128]],
                            compare_op=ALU.is_equal, fill=0.0, base=0,
                            channel_multiplier=-1).then_inc(s_pool.h, 1)
            for src, dst in ((gs_d, gs_s), (us_d, us_s), (ds_d, ds_s)):
                g.dma_start(out=dst[:], in_=src).then_inc(s_dsc.h, 16)

        sc1 = nc.enter_named_scope('p1_ln', False)

        @block.sync
        def _(s):
            def xn_store(j):
                s.wait_ge(s_lnV.h, 5 * j + 5)
                s.dma_start(out=xn_dram[j * 128:(j + 1) * 128, :],
                            in_=xnb[j % 2][:]).then_inc(s_dxn[j % 2].h, 16)
            for i in range(NT):
                if i >= 2:
                    xn_store(i - 2)
                s.dma_start(out=xt[i % 2][:],
                            in_=x_d[i * 128:(i + 1) * 128, :]).then_inc(s_dxl[i % 2].h, 16)
            xn_store(NT - 2)
            xn_store(NT - 1)

        @block.vector
        def _(v):
            v.wait_ge(s_pool.h, 4)
            for i in range(NT):
                c0 = (i % 2) * 8
                w16(v, s_dxl, i)
                v.reduce_sum(stat[:, c0:c0 + 1], xt[i % 2][:], axis=AX.X).then_inc(s_lnV.h, 1)
                v.wait_ge(s_lnV.h, 5 * i + 1)
                v.tensor_scalar(out=stat[:, c0 + 1:c0 + 2], in0=stat[:, c0:c0 + 1],
                                scalar1=-INV_D, scalar2=None, op0=ALU.mult).then_inc(s_lnV.h, 1)
                v.wait_ge(s_lnV.h, 5 * i + 2)
                v.tensor_tensor(out=stat[:, c0 + 2:c0 + 3], in0=stat[:, c0 + 1:c0 + 2],
                                in1=stat[:, c0 + 1:c0 + 2], op=ALU.mult).then_inc(s_lnV.h, 1)
                v.wait_ge(s_lnA.h, 3 * i + 1)
                v.wait_ge(s_lnV.h, 5 * i + 3)
                v.tensor_scalar(out=stat[:, c0 + 4:c0 + 5], in0=stat[:, c0 + 3:c0 + 4],
                                scalar1=INV_D, scalar2=stat[:, c0 + 2:c0 + 3],
                                op0=ALU.mult, op1=ALU.subtract).then_inc(s_lnV.h, 1)
                v.wait_ge(s_lnA.h, 3 * i + 3)
                if i >= 2:
                    w16(v, s_dxn, i - 2)
                v.tensor_scalar(out=xnb[i % 2][:], in0=xt[i % 2][:],
                                scalar1=stat[:, c0 + 1:c0 + 2],
                                scalar2=stat[:, c0 + 6:c0 + 7],
                                op0=ALU.add, op1=ALU.mult).then_inc(s_lnV.h, 1)

        @block.scalar
        def _(sc):
            sc.wait_ge(s_pool.h, 4)
            for i in range(NT):
                c0 = (i % 2) * 8
                w16(sc, s_dxl, i)
                if i >= 2:
                    sc.wait_ge(s_lnV.h, 5 * (i - 2) + 4)
                sc.activation(out=sqscr[:], in_=xt[i % 2][:], func=AF.Square,
                              bias=0.0, scale=1.0,
                              accum_out=stat[:, c0 + 3:c0 + 4]).then_inc(s_lnA.h, 1)
                sc.wait_ge(s_lnV.h, 5 * i + 4)
                sc.activation(out=stat[:, c0 + 5:c0 + 6], in_=stat[:, c0 + 4:c0 + 5],
                              func=AF.Ln, bias=LN_EPS, scale=1.0).then_inc(s_lnA.h, 1)
                sc.wait_ge(s_lnA.h, 3 * i + 2)
                sc.activation(out=stat[:, c0 + 6:c0 + 7], in_=stat[:, c0 + 5:c0 + 6],
                              func=AF.Exp, bias=0.0, scale=-0.5).then_inc(s_lnA.h, 1)

        nc.leave_named_scope('p1_ln', sc1[0], False)

        for cm in reversed(p1_cms):
            cm.__exit__(None, None, None)
        xnT_cm = nc.sbuf_tensor("xnT", [128, NDC * T], BF16)
        xnT = xnT_cm.__enter__()
        p3_cms = [nc.sbuf_tensor("wq_b0", [128, D], BF16, side="right"),
                  nc.sbuf_tensor("wq_b1", [128, D], BF16, side="right"),
                  nc.sbuf_tensor("qT_h", [128, T], BF16, side="right"),
                  nc.sbuf_tensor("PT", [128, NT * 512], BF16, side="right"),
                  nc.sbuf_tensor("oT_strip0", [128, T], BF16, side="right"),
                  nc.sbuf_tensor("oT_strip1", [128, T], BF16, side="right")]
        wq_b0, wq_b1, qT_h, PT, oT_s0, oT_s1 = [cm.__enter__() for cm in p3_cms]
        wq_b = [wq_b0, wq_b1]
        oT_strip = [oT_s0, oT_s1]
        vT_tmp = qT_h
        wkv_b = PT
        ps_big = e(nc.psum_tensor("ps_big", [128, 2048], F32))
        pt_cms = [nc.psum_tensor("ps_t0", [128, 1024], BF16),
                  nc.psum_tensor("ps_t1", [128, 1024], BF16)]
        ps_t0, ps_t1 = [cm.__enter__() for cm in pt_cms]

        @block.sync
        def _(s):
            wall16(s, s_dxn, NT)
            for dc in range(NDC):
                s.dma_start_transpose(xnT[:, dc * T:(dc + 1) * T],
                                      xn_dram[:, dc * 128:(dc + 1) * 128]).then_inc(s_dxt.h, 16)

        @block.gpsimd
        def _(g):
            wall16(g, s_dxl, NT)
            wall16(g, s_dxn, NT)
            g.wait_ge(s_lnV.h, 5 * NT)
            g.wait_ge(s_lnA.h, 3 * NT)
            g.dma_start(out=wkv_b[:, 0:D], in_=wk_d).then_inc(s_dwkv.h, 16)
            g.dma_start(out=wkv_b[:, D:2 * D], in_=wv_d).then_inc(s_dwkv.h, 16)
            for h in range(NH):
                if h >= 2:
                    g.wait_ge(s_q.h, h - 1)
                g.dma_start(out=wq_b[h % 2][:], in_=wq_d[h]).then_inc(s_dwq[h % 2].h, 16)

        sc2 = nc.enter_named_scope('p2_kv', False)

        @block.tensor
        def _(t):
            t.wait_ge(s_pool.h, 4)
            t.wait_ge(s_dwkv.h, 32)
            t.wait_ge(s_dxt.h, 16 * NDC)
            for which in range(2):
                if which == 1:
                    t.wait_ge(s_kve.h, 1)
                for dc in range(NDC):
                    for m in range(4):
                        ins = t.matmul(
                            ps_big[:, m * 512:(m + 1) * 512],
                            wkv_b[:, which * D + dc * 128:which * D + (dc + 1) * 128],
                            xnT[:, dc * T + m * 512:dc * T + (m + 1) * 512],
                            start=(dc == 0), stop=(dc == NDC - 1))
                ins.then_inc(s_kv.h, 1)
            for kb in range(NT):
                if kb == 0:
                    t.wait_ge(s_kve.h, 2)
                if kb >= 2:
                    t.wait_ge(s_vte.h, kb - 1)
                pst = ps_t0 if kb % 2 == 0 else ps_t1
                t.transpose(pst[:, 0:128], vT_tmp[:, kb * 128:(kb + 1) * 128],
                            ident[:]).then_inc(s_vtr.h, 1)

        @block.vector
        def _(v):
            v.wait_ge(s_pool.h, 4)
            for which in range(2):
                v.wait_ge(s_kv.h, which + 1)
                dst = kT if which == 0 else vT_tmp
                v.tensor_copy(dst[:], ps_big[:]).then_inc(s_kve.h, 1)
            for kb in range(NT):
                v.wait_ge(s_vtr.h, kb + 1)
                pst = ps_t0 if kb % 2 == 0 else ps_t1
                v.tensor_copy(vtok[:, kb * 128:(kb + 1) * 128],
                              pst[:, 0:128]).then_inc(s_vte.h, 1)

        nc.leave_named_scope('p2_kv', sc2[0], False)

        for cm in reversed(pt_cms):
            cm.__exit__(None, None, None)
        p3ps_cms = [nc.psum_tensor("psS", [128, 512], F32),
                    nc.psum_tensor("psO0", [128, 512], F32),
                    nc.psum_tensor("psO1", [128, 512], F32)]
        psS, psO0, psO1 = [cm.__enter__() for cm in p3ps_cms]
        psO = [psO0, psO1]

        sc3 = nc.enter_named_scope('p3_attn', False)

        CH0 = [0]
        for _qq in range(NQ):
            CH0.append(CH0[-1] + 4 * _qq + 4)
        NCH = CH0[-1]

        @block.tensor
        def _(t):
            t.wait_ge(s_pool.h, 4)
            for h in range(NH):
                w16(t, s_dwq, h)
                if h == 0:
                    t.wait_ge(s_vte.h, NT)
                else:
                    t.wait_ge(s_exp.h, h * NCH)
                    t.wait_ge(s_ive.h, h * NQ)
                for dc in range(NDC):
                    for m in range(4):
                        ins = t.matmul(
                            ps_big[:, m * 512:(m + 1) * 512],
                            wq_b[h % 2][:, dc * 128:(dc + 1) * 128],
                            xnT[:, dc * T + m * 512:dc * T + (m + 1) * 512],
                            start=(dc == 0), stop=(dc == NDC - 1))
                ins.then_inc(s_q.h, 1)

                def ones_pv(cp, h=h):
                    qq = next(q for q in range(NQ) if CH0[q] <= cp < CH0[q + 1])
                    j = cp - CH0[qq]
                    last = (j == 4 * qq + 3)
                    t.wait_ge(s_exp.h, h * NCH + cp + 1)
                    if j == 0:
                        t.wait_ge(s_iv.h, h * NQ + qq)
                        t.wait_ge(s_pso.h, h * NQ + max(0, qq - 1))
                    q0 = max(0, j * 128 - qq * 512)
                    ins = t.matmul(psS[0:1, q0:512], ones_c[:],
                                   PT[:, j * 512 + q0:(j + 1) * 512],
                                   start=(j == 0), stop=last)
                    if last:
                        ins.then_inc(s_on.h, 1)
                    ins = t.matmul(psO[qq % 2][:, q0:512],
                                   vtok[:, j * 128:(j + 1) * 128],
                                   PT[:, j * 512 + q0:(j + 1) * 512],
                                   start=(j == 0), stop=last)
                    if last:
                        ins.then_inc(s_pv.h, 1)

                for qq in range(NQ):
                    for j in range(4 * qq + 4):
                        c = CH0[qq] + j
                        if c >= 3:
                            t.wait_ge(s_exp.h, h * NCH + c - 2)
                        else:
                            t.wait_ge(s_qe.h, h + 1)
                        ins = t.matmul(
                            ps_big[:, (c % 3) * 512:(c % 3 + 1) * 512],
                            kT[:, j * 128:(j + 1) * 128],
                            qT_h[:, qq * 512:(qq + 1) * 512],
                            start=True, stop=True)
                        ins.then_inc(s_st.h, 1)
                        if c >= 2:
                            ones_pv(c - 2)
                        if qq >= 1 and j == 1:
                            t.wait_ge(s_iv.h, h * NQ + qq)
                            t.wait_ge(s_ive.h, h * NQ + qq - 1)
                            t.matmul(ps_big[:, 1536:2048], ones_r[:],
                                     inv_sb[:], start=True,
                                     stop=True).then_inc(s_ib.h, 1)
                ones_pv(NCH - 2)
                ones_pv(NCH - 1)
                t.wait_ge(s_iv.h, h * NQ + NQ)
                t.wait_ge(s_ive.h, h * NQ + NQ - 1)
                t.matmul(ps_big[:, 1536:2048], ones_r[:],
                         inv_sb[:], start=True, stop=True).then_inc(s_ib.h, 1)

        @block.scalar
        def _(sc):
            sc.wait_ge(s_pool.h, 4)
            for h in range(NH):
                sc.wait_ge(s_q.h, h + 1)
                sc.activation(out=qT_h[:], in_=ps_big[:], func=AF.Copy,
                              bias=0.0, scale=1.0).then_inc(s_qe.h, 1)
                for qq in range(NQ):
                    for j in range(4 * qq + 4):
                        c = CH0[qq] + j
                        sc.wait_ge(s_st.h, h * NCH + c + 1)
                        if j >= 4 * qq:
                            k = j - 4 * qq
                            sc.wait_ge(s_msk.h, h * 16 + qq * 4 + k + 1)
                        sc.wait_ge(s_pv.h, h * NQ + qq)
                        sc.activation(
                            out=PT[:, j * 512:(j + 1) * 512],
                            in_=ps_big[:, (c % 3) * 512:(c % 3 + 1) * 512],
                            func=AF.Exp, bias=EXP_BIAS,
                            scale=C_SCALE).then_inc(s_exp.h, 1)

        @block.vector
        def _(v):
            v.wait_ge(s_pool.h, 4)
            for h in range(NH):
                for qq in range(NQ):
                    if qq >= 1:
                        qp = qq - 1
                        v.wait_ge(s_on.h, h * NQ + qp + 1)
                        v.wait_ge(s_ib.h, h * NQ + qp)
                        v.reciprocal(inv_sb[:], psS[0:1, :]).then_inc(s_iv.h, 1)
                        v.wait_ge(s_ib.h, h * NQ + qp + 1)
                        v.tensor_copy(invb[:], ps_big[:, 1536:2048]).then_inc(s_ive.h, 1)
                        v.wait_ge(s_pv.h, h * NQ + qp + 1)
                        v.wait_ge(s_ive.h, h * NQ + qp + 1)
                        if h >= 2 and qp == 0:
                            w16(v, s_doS, h - 2)
                        v.tensor_tensor(out=oT_strip[h % 2][:, qp * 512:(qp + 1) * 512],
                                        in0=psO[qp % 2][:], in1=invb[:],
                                        op=ALU.mult).then_inc(s_pso.h, 1)
                    for k in range(4):
                        c = CH0[qq] + 4 * qq + k
                        v.wait_ge(s_st.h, h * NCH + c + 1)
                        sl = ps_big[:, (c % 3) * 512:(c % 3 + 1) * 512]
                        v.tensor_tensor(
                            out=sl, in0=sl,
                            in1=maskM[:, 384 - 128 * k:896 - 128 * k],
                            op=ALU.add).then_inc(s_msk.h, 1)
                qp = NQ - 1
                v.wait_ge(s_on.h, h * NQ + qp + 1)
                v.wait_ge(s_ib.h, h * NQ + qp)
                v.reciprocal(inv_sb[:], psS[0:1, :]).then_inc(s_iv.h, 1)
                v.wait_ge(s_ib.h, h * NQ + qp + 1)
                v.tensor_copy(invb[:], ps_big[:, 1536:2048]).then_inc(s_ive.h, 1)
                v.wait_ge(s_pv.h, h * NQ + qp + 1)
                v.wait_ge(s_ive.h, h * NQ + qp + 1)
                v.tensor_tensor(out=oT_strip[h % 2][:, qp * 512:(qp + 1) * 512],
                                in0=psO[qp % 2][:], in1=invb[:],
                                op=ALU.mult).then_inc(s_pso.h, 1)

        @block.sync
        def _(s):
            for h in range(NH):
                s.wait_ge(s_pso.h, h * NQ + NQ)
                s.dma_start(out=oT_dram[h], in_=oT_strip[h % 2][:]).then_inc(s_doS[h % 2].h, 16)

        nc.leave_named_scope('p3_attn', sc3[0], False)

        for cm in reversed(p3_cms):
            cm.__exit__(None, None, None)
        p35_cms = [nc.sbuf_tensor("Wg_b0", [128, D], BF16, side="right"),
                   nc.sbuf_tensor("Wg_b1", [128, D], BF16, side="right"),
                   nc.sbuf_tensor("Wu_b0", [128, D], BF16, side="right"),
                   nc.sbuf_tensor("Wu_b1", [128, D], BF16, side="right"),
                   nc.sbuf_tensor("silu_g", [128, T], BF16, side="right"),
                   nc.sbuf_tensor("h_buf0", [128, T], BF16, side="right"),
                   nc.sbuf_tensor("h_buf1", [128, T], BF16, side="right")]
        Wg0, Wg1, Wu0, Wu1, silu_g, hb0, hb1 = [cm.__enter__() for cm in p35_cms]
        Wg_b = [Wg0, Wg1]
        Wu_b = [Wu0, Wu1]
        h_buf = [hb0, hb1]
        for cm in reversed(p3ps_cms):
            cm.__exit__(None, None, None)
        ps_up_cm = nc.psum_tensor("ps_up", [128, 2048], F32)
        ps_up = ps_up_cm.__enter__()

        @block.gpsimd
        def _(g):
            g.wait_ge(s_exp.h, NH * 40)
            g.wait_ge(s_iv.h, NH * NQ)
            g.wait_ge(s_ive.h, NH * NQ)
            g.wait_ge(s_pso.h, NH * NQ)
            wall16(g, s_doS, NH)
            wall16(g, s_dwq, NH)
            for fc in range(NFC):
                if fc >= 2:
                    g.wait_ge(s_gmm.h, fc - 1)
                g.dma_start(out=Wg_b[fc % 2][:], in_=gw_d[fc]).then_inc(s_dgw[fc % 2].h, 16)
                if fc >= 2:
                    g.wait_ge(s_umm.h, fc - 1)
                g.dma_start(out=Wu_b[fc % 2][:], in_=uw_d[fc]).then_inc(s_duw[fc % 2].h, 16)

        sc35 = nc.enter_named_scope('p35_gateup', False)

        @block.tensor
        def _(t):
            for fc in range(NFC):
                w16(t, s_dgw, fc)
                if fc == 0:
                    t.wait_ge(s_exp.h, NH * 40)
                    t.wait_ge(s_msk.h, NH * 16)
                    t.wait_ge(s_iv.h, NH * NQ)
                    t.wait_ge(s_ive.h, NH * NQ)
                    t.wait_ge(s_pso.h, NH * NQ)
                else:
                    t.wait_ge(s_silu.h, fc)
                for dc in range(NDC):
                    for m in range(4):
                        ins = t.matmul(
                            ps_big[:, m * 512:(m + 1) * 512],
                            Wg_b[fc % 2][:, dc * 128:(dc + 1) * 128],
                            xnT[:, dc * T + m * 512:dc * T + (m + 1) * 512],
                            start=(dc == 0), stop=(dc == NDC - 1))
                ins.then_inc(s_gmm.h, 1)
                w16(t, s_duw, fc)
                if fc >= 1:
                    t.wait_ge(s_hm.h, fc)
                for dc in range(NDC):
                    for m in range(4):
                        ins = t.matmul(
                            ps_up[:, m * 512:(m + 1) * 512],
                            Wu_b[fc % 2][:, dc * 128:(dc + 1) * 128],
                            xnT[:, dc * T + m * 512:dc * T + (m + 1) * 512],
                            start=(dc == 0), stop=(dc == NDC - 1))
                ins.then_inc(s_umm.h, 1)

        @block.scalar
        def _(sc):
            sc.wait_ge(s_dsc.h, 48)
            for fc in range(NFC):
                sc.wait_ge(s_gmm.h, fc + 1)
                if fc >= 1:
                    sc.wait_ge(s_hm.h, fc)
                sc.activation(out=silu_g[:], in_=ps_big[:], func=AF.Silu,
                              bias=0.0, scale=gs_s[:, fc:fc + 1]).then_inc(s_silu.h, 1)

        @block.vector
        def _(v):
            for fc in range(NFC):
                v.wait_ge(s_umm.h, fc + 1)
                v.wait_ge(s_silu.h, fc + 1)
                if fc >= 2:
                    w16(v, s_dhS, fc - 2)
                v.scalar_tensor_tensor(
                    out=h_buf[fc % 2][:], in0=ps_up[:],
                    scalar=us_s[:, fc:fc + 1], in1=silu_g[:],
                    op0=ALU.mult, op1=ALU.mult).then_inc(s_hm.h, 1)

        @block.sync
        def _(s):
            for fc in range(NFC):
                s.wait_ge(s_hm.h, fc + 1)
                s.dma_start(out=h_dram[fc * 128:(fc + 1) * 128, :],
                            in_=h_buf[fc % 2][:]).then_inc(s_dhS[fc % 2].h, 16)

        nc.leave_named_scope('p35_gateup', sc35[0], False)

        for cm in reversed(p35_cms):
            cm.__exit__(None, None, None)
        xnT_cm.__exit__(None, None, None)
        p4_cms = [nc.sbuf_tensor("oT_all", [128, NH * T], BF16, side="right"),
                  nc.sbuf_tensor("wo_b0", [128, D], BF16, side="right"),
                  nc.sbuf_tensor("wo_b1", [128, D], BF16, side="right"),
                  nc.sbuf_tensor("at_b0", [128, T], BF16, side="right"),
                  nc.sbuf_tensor("at_b1", [128, T], BF16, side="right")]
        oT_all, wo_b0, wo_b1, at_b0, at_b1 = [cm.__enter__() for cm in p4_cms]
        wo_b = [wo_b0, wo_b1]
        at_b = [at_b0, at_b1]

        @block.gpsimd
        def _(g):
            g.wait_ge(s_umm.h, NFC)
            g.wait_ge(s_hm.h, NFC)
            wall16(g, s_dhS, NFC)
            wall16(g, s_dgw, NFC)
            wall16(g, s_duw, NFC)
            for do in range(NDC):
                if do >= 2:
                    g.wait_ge(s_omm.h, 2 * (do - 1))
                g.dma_start(out=wo_b[do % 2][:], in_=wo_d[do]).then_inc(s_dwo[do % 2].h, 16)

        sc4 = nc.enter_named_scope('p4_oproj', False)

        @block.sync
        def _(s):
            s.wait_ge(s_umm.h, NFC)
            s.wait_ge(s_hm.h, NFC)
            wall16(s, s_dhS, NFC)
            wall16(s, s_dgw, NFC)
            wall16(s, s_duw, NFC)
            for di in range(0, NH, 2):
                s.dma_start(out=oT_all[:, di * T:(di + 1) * T],
                            in_=oT_dram[di]).then_inc(s_dolS.h, 16)

        @block.scalar
        def _(sc):
            sc.wait_ge(s_umm.h, NFC)
            sc.wait_ge(s_hm.h, NFC)
            wall16(sc, s_dhS, NFC)
            wall16(sc, s_dgw, NFC)
            wall16(sc, s_duw, NFC)
            for di in range(1, NH, 2):
                sc.dma_start(out=oT_all[:, di * T:(di + 1) * T],
                             in_=oT_dram[di]).then_inc(s_dolC.h, 16)

        @block.tensor
        def _(t):
            for do in range(NDC):
                w16(t, s_dwo, do)
                if do == 0:
                    t.wait_ge(s_silu.h, NFC)
                    t.wait_ge(s_hm.h, NFC)
                    t.wait_ge(s_dolS.h, 16 * 16)
                    t.wait_ge(s_dolC.h, 16 * 16)
                for half in range(2):
                    q = do * 2 + half
                    psA = ps_big if (q % 4) < 2 else ps_up
                    c0 = (q % 2) * 1024
                    if q >= 4:
                        t.wait_ge(s_oev.h, q - 3)
                    for di in range(NH):
                        for m in range(2):
                            ins = t.matmul(
                                psA[:, c0 + m * 512:c0 + (m + 1) * 512],
                                wo_b[do % 2][:, di * 128:(di + 1) * 128],
                                oT_all[:, di * T + half * 1024 + m * 512:
                                       di * T + half * 1024 + (m + 1) * 512],
                                start=(di == 0), stop=(di == NH - 1))
                    ins.then_inc(s_omm.h, 1)

        @block.vector
        def _(v):
            for do in range(NDC):
                for half in range(2):
                    q = do * 2 + half
                    psA = ps_big if (q % 4) < 2 else ps_up
                    c0 = (q % 2) * 1024
                    v.wait_ge(s_omm.h, q + 1)
                    if half == 0 and do >= 2:
                        w16(v, s_daS, do - 2)
                    v.tensor_copy(at_b[do % 2][:, half * 1024:(half + 1) * 1024],
                                  psA[:, c0:c0 + 1024]).then_inc(s_oev.h, 1)

        @block.sync
        def _(s):
            for do in range(NDC):
                s.wait_ge(s_oev.h, 2 * (do + 1))
                s.dma_start(out=attnT_dram[do * 128:(do + 1) * 128, :],
                            in_=at_b[do % 2][:]).then_inc(s_daS[do % 2].h, 16)

        nc.leave_named_scope('p4_oproj', sc4[0], False)

        for cm in reversed(p4_cms):
            cm.__exit__(None, None, None)
        NPMAX = max(np_ for _f, np_ in PARTS)
        p5_cms = [nc.sbuf_tensor("h_part", [128, NPMAX * T], BF16, side="right"),
                  nc.sbuf_tensor("Wd_b0", [128, NPMAX * 128], BF16, side="right"),
                  nc.sbuf_tensor("Wd_b1", [128, NPMAX * 128], BF16, side="right"),
                  nc.sbuf_tensor("dp_in0", [128, T], F32, side="right"),
                  nc.sbuf_tensor("dp_in1", [128, T], F32, side="right"),
                  nc.sbuf_tensor("dp_out0", [128, T], F32, side="right"),
                  nc.sbuf_tensor("dp_out1", [128, T], F32, side="right"),
                  nc.sbuf_tensor("dn_b0", [128, T], BF16, side="right"),
                  nc.sbuf_tensor("dn_b1", [128, T], BF16, side="right")]
        (h_part, Wd_b0, Wd_b1, dp_in0, dp_in1,
         dp_out0, dp_out1, dn_b0, dn_b1) = [cm.__enter__() for cm in p5_cms]
        Wd_b = [Wd_b0, Wd_b1]
        dp_in = [dp_in0, dp_in1]
        dp_out = [dp_out0, dp_out1]
        dn_b = [dn_b0, dn_b1]

        hl_base = {}
        _c = 0
        for p, (fc0, npart) in enumerate(PARTS):
            hl_base[p] = _c
            _c += npart

        @block.gpsimd
        def _(g):
            g.wait_ge(s_omm.h, 2 * NDC)
            g.wait_ge(s_oev.h, 2 * NDC)
            wall16(g, s_daS, NDC)
            wall16(g, s_dwo, NDC)
            g.wait_ge(s_dolS.h, 16 * 16)
            g.wait_ge(s_dolC.h, 16 * 16)
            for p, (fc0, npart) in enumerate(PARTS):
                for dm in range(NDC):
                    if p * 32 + dm >= 2:
                        g.wait_ge(s_dmm.h, 2 * (p * 32 + dm - 1))
                    g.dma_start(out=Wd_b[dm % 2][:, 0:npart * 128],
                                in_=dw_d[dm][:, fc0 * 128:(fc0 + npart) * 128]).then_inc(s_ddw[(p * 32 + dm) % 2].h, 16)

        sc5 = nc.enter_named_scope('p5_down', False)

        def dF2(p, dm, half):
            return 2 * (p * 32 + dm) + half + 1

        @block.sync
        def _(s):
            for p, (fc0, npart) in enumerate(PARTS):
                if p == 0:
                    s.wait_ge(s_omm.h, 2 * NDC)
                    s.wait_ge(s_oev.h, 2 * NDC)
                    wall16(s, s_daS, NDC)
                    wall16(s, s_dhS, NFC)
                    wall16(s, s_dwo, NDC)
                    s.wait_ge(s_dolS.h, 16 * 16)
                    s.wait_ge(s_dolC.h, 16 * 16)
                else:
                    s.wait_ge(s_dmm.h, 64 * p)
                for j in range(npart):
                    fc = fc0 + j
                    s.dma_start(out=h_part[:, j * T:(j + 1) * T],
                                in_=h_dram[fc * 128:(fc + 1) * 128, :]).then_inc(s_dhL.h, 16)
                if p < 2:
                    for dm in range(NDC):
                        s.wait_ge(s_dev.h, dF2(p, dm, 1))
                        s.dma_start(out=dpart_dram[dm * 128:(dm + 1) * 128, :],
                                    in_=dp_out[dm % 2][:]).then_inc(s_dpS[(p * 32 + dm) % 2].h, 16)
            for dm in range(NDC):
                s.wait_ge(s_dev.h, dF2(2, dm, 1))
                s.dma_start(out=downT_dram[dm * 128:(dm + 1) * 128, :],
                            in_=dn_b[dm % 2][:]).then_inc(s_dnS[dm % 2].h, 16)

        @block.tensor
        def _(t):
            for p, (fc0, npart) in enumerate(PARTS):
                for dm in range(NDC):
                    w16(t, s_ddw, p * 32 + dm)
                    if dm == 0:
                        t.wait_ge(s_dhL.h, 16 * (hl_base[p] + npart))
                    for half in range(2):
                        q = 2 * (p * 32 + dm) + half
                        psA = ps_big if (q % 4) < 2 else ps_up
                        c0 = (q % 2) * 1024
                        if q >= 4:
                            t.wait_ge(s_dev.h, q - 3)
                        for j in range(npart):
                            for m in range(2):
                                ins = t.matmul(
                                    psA[:, c0 + m * 512:c0 + (m + 1) * 512],
                                    Wd_b[dm % 2][:, j * 128:(j + 1) * 128],
                                    h_part[:, j * T + half * 1024 + m * 512:
                                           j * T + half * 1024 + (m + 1) * 512],
                                    start=(j == 0), stop=(j == npart - 1))
                        ins.then_inc(s_dmm.h, 1)

        @block.scalar
        def _(sc):
            for p in (1, 2):
                for dm in range(NDC):
                    w16(sc, s_dpS, (p - 1) * 32 + dm)
                    if dm >= 2:
                        sc.wait_ge(s_dev.h, dF2(p, dm - 2, 1))
                    elif p == 2:
                        sc.wait_ge(s_dev.h, dF2(1, NDC - 2 + dm, 1))
                    sc.dma_start(out=dp_in[dm % 2][:],
                                 in_=dpart_dram[dm * 128:(dm + 1) * 128, :]).then_inc(
                        s_dpL[((p - 1) * 32 + dm) % 2].h, 16)

        @block.vector
        def _(v):
            for p, (fc0, npart) in enumerate(PARTS):
                for dm in range(NDC):
                    for half in range(2):
                        q = 2 * (p * 32 + dm) + half
                        psA = ps_big if (q % 4) < 2 else ps_up
                        c0 = (q % 2) * 1024
                        hs = slice(half * 1024, (half + 1) * 1024)
                        v.wait_ge(s_dmm.h, q + 1)
                        if p == 0:
                            if half == 0 and dm >= 2:
                                w16(v, s_dpS, dm - 2)
                            v.tensor_copy(dp_out[dm % 2][:, hs],
                                          psA[:, c0:c0 + 1024]).then_inc(s_dev.h, 1)
                        elif p == 1:
                            if half == 0:
                                w16(v, s_dpL, dm)
                                if dm >= 2:
                                    w16(v, s_dpS, 32 + dm - 2)
                                else:
                                    w16(v, s_dpS, NDC - 2 + dm)
                            v.tensor_tensor(out=dp_out[dm % 2][:, hs],
                                            in0=psA[:, c0:c0 + 1024],
                                            in1=dp_in[dm % 2][:, hs],
                                            op=ALU.add).then_inc(s_dev.h, 1)
                        else:
                            if half == 0:
                                w16(v, s_dpL, 32 + dm)
                                if dm >= 2:
                                    w16(v, s_dnS, dm - 2)
                                    wall16(v, s_dpS, 64)
                                else:
                                    w16(v, s_dpS, 32 + NDC - 2 + dm)
                            v.tensor_tensor(out=dp_out[dm % 2][:, hs],
                                            in0=psA[:, c0:c0 + 1024],
                                            in1=dp_in[dm % 2][:, hs],
                                            op=ALU.add).then_inc(s_dv2.h, 1)
                            v.wait_ge(s_dv2.h, 2 * dm + half + 1)
                            v.tensor_scalar(out=dn_b[dm % 2][:, hs],
                                            in0=dp_out[dm % 2][:, hs],
                                            scalar1=ds_s[:, dm:dm + 1], scalar2=None,
                                            op0=ALU.mult).then_inc(s_dev.h, 1)

        nc.leave_named_scope('p5_down', sc5[0], False)

        for cm in reversed(p5_cms):
            cm.__exit__(None, None, None)
        p6_cms = [nc.sbuf_tensor("x6_0", [128, D], F32, side="right"),
                  nc.sbuf_tensor("x6_1", [128, D], F32, side="right"),
                  nc.sbuf_tensor("a6_0", [128, D], BF16, side="right"),
                  nc.sbuf_tensor("a6_1", [128, D], BF16, side="right"),
                  nc.sbuf_tensor("d6_0", [128, D], BF16, side="right"),
                  nc.sbuf_tensor("d6_1", [128, D], BF16, side="right"),
                  nc.sbuf_tensor("o6_0", [128, D], F32, side="right"),
                  nc.sbuf_tensor("o6_1", [128, D], F32, side="right")]
        x6_0, x6_1, a6_0, a6_1, d6_0, d6_1, o6_0, o6_1 = [cm.__enter__() for cm in p6_cms]
        x6 = [x6_0, x6_1]
        a6 = [a6_0, a6_1]
        d6 = [d6_0, d6_1]
        o6 = [o6_0, o6_1]

        sc6 = nc.enter_named_scope('p6_final', False)

        @block.sync
        def _(s):
            s.wait_ge(s_dmm.h, 192)
            s.wait_ge(s_dev.h, 192)
            wall16(s, s_dnS, NDC)
            wall16(s, s_dpS, 64)
            wall16(s, s_ddw, 96)
            s.wait_ge(s_dhL.h, 16 * 86)
            wall16(s, s_dpL, 64)
            def out_store(j):
                s.wait_ge(s_sum.h, 2 * (j + 1))
                s.dma_start(out=out_d[j * 128:(j + 1) * 128, :],
                            in_=o6[j % 2][:]).then_inc(s_dout[j % 2].h, 16)
            for i in range(NT):
                if i >= 2:
                    out_store(i - 2)
                s.dma_start_transpose(a6[i % 2][:],
                                      attnT_dram[:, i * 128:(i + 1) * 128]).then_inc(s_d6[i % 2].h, 16)
                s.dma_start_transpose(d6[i % 2][:],
                                      downT_dram[:, i * 128:(i + 1) * 128]).then_inc(s_d6[i % 2].h, 16)
            out_store(NT - 2)
            out_store(NT - 1)
            wall16(s, s_dout, NT)

        @block.scalar
        def _(sc):
            sc.wait_ge(s_dmm.h, 192)
            sc.wait_ge(s_dev.h, 192)
            wall16(sc, s_dnS, NDC)
            wall16(sc, s_ddw, 96)
            sc.wait_ge(s_dhL.h, 16 * 86)
            wall16(sc, s_dpL, 64)
            for i in range(NT):
                if i >= 2:
                    sc.wait_ge(s_sum.h, 2 * (i - 1))
                sc.dma_start(out=x6[i % 2][:],
                             in_=x_d[i * 128:(i + 1) * 128, :]).then_inc(s_dx6[i % 2].h, 16)

        @block.vector
        def _(v):
            for i in range(NT):
                v.wait_ge(s_d6[i % 2].h, 32 * (i // 2 + 1))
                w16(v, s_dx6, i)
                if i >= 2:
                    w16(v, s_dout, i - 2)
                v.tensor_tensor(out=o6[i % 2][:], in0=a6[i % 2][:],
                                in1=d6[i % 2][:], op=ALU.add).then_inc(s_sum.h, 1)
                v.wait_ge(s_sum.h, 2 * i + 1)
                v.tensor_tensor(out=o6[i % 2][:], in0=o6[i % 2][:],
                                in1=x6[i % 2][:], op=ALU.add).then_inc(s_sum.h, 1)

        nc.leave_named_scope('p6_final', sc6[0], False)

        for cm in reversed(p6_cms):
            cm.__exit__(None, None, None)
        ps_up_cm.__exit__(None, None, None)

    return nc


def _host_layout(wq, wk, wv, wo, gw, uw, dw, gs, us, ds):
    wq_p = np.ascontiguousarray(
        wq.reshape(NDC, 128, NH, 128).transpose(2, 1, 0, 3).reshape(NH, 128, D))
    wo_p = np.ascontiguousarray(
        wo.reshape(NDC, 128, NDC, 128).transpose(2, 1, 0, 3).reshape(NDC, 128, D))
    wk_p = np.ascontiguousarray(
        wk.reshape(NDC, 128, 128).transpose(1, 0, 2).reshape(128, D))
    wv_p = np.ascontiguousarray(
        wv.reshape(NDC, 128, 128).transpose(1, 0, 2).reshape(128, D))
    gw_p = np.ascontiguousarray(
        gw.reshape(NFC, 128, NDC, 128).transpose(0, 3, 2, 1).reshape(NFC, 128, D))
    uw_p = np.ascontiguousarray(
        uw.reshape(NFC, 128, NDC, 128).transpose(0, 3, 2, 1).reshape(NFC, 128, D))
    dw_p = np.ascontiguousarray(
        dw.reshape(NDC, 128, NFC, 128).transpose(0, 3, 2, 1).reshape(NDC, 128, DFF))
    gs_p = np.ascontiguousarray(gs.reshape(NFC, 128).T)
    us_p = np.ascontiguousarray(us.reshape(NFC, 128).T)
    ds_p = np.ascontiguousarray(ds.reshape(NDC, 128).T)
    return wq_p, wk_p, wv_p, wo_p, gw_p, uw_p, dw_p, gs_p, us_p, ds_p


def kernel(L0_attn_x, L0_attn_wq, L0_attn_wk, L0_attn_wv, L0_attn_wo,
           L0_ffn_gate_w, L0_ffn_gate_s, L0_ffn_up_w, L0_ffn_up_s,
           L0_ffn_down_w, L0_ffn_down_s):
    x = np.ascontiguousarray(np.asarray(L0_attn_x, dtype=np.float32))
    (wq_p, wk_p, wv_p, wo_p, gw_p, uw_p, dw_p, gs_p, us_p, ds_p) = _host_layout(
        np.asarray(L0_attn_wq, np.float32), np.asarray(L0_attn_wk, np.float32),
        np.asarray(L0_attn_wv, np.float32), np.asarray(L0_attn_wo, np.float32),
        np.asarray(L0_ffn_gate_w, np.int32), np.asarray(L0_ffn_up_w, np.int32),
        np.asarray(L0_ffn_down_w, np.int32), np.asarray(L0_ffn_gate_s, np.float32),
        np.asarray(L0_ffn_up_s, np.float32), np.asarray(L0_ffn_down_s, np.float32))

    nc = build_nc()
    in_maps = []
    for c in range(8):
        in_maps.append(dict(
            x=np.ascontiguousarray(x[c]),
            wq_p=wq_p, wk_p=wk_p, wv_p=wv_p, wo_p=wo_p,
            gw_p=gw_p, uw_p=uw_p, dw_p=dw_p,
            gs_p=gs_p, us_p=us_p, ds_p=ds_p,
        ))
    res = run_bass_kernel_spmd(nc, in_maps, core_ids=list(range(8)))
    out = np.stack([res.results[c]["out"] for c in range(8)], axis=0)
    return out


# version 08
# speedup vs baseline: 1.1080x; 1.1080x over previous
import numpy as np
from contextlib import ExitStack

import concourse.bass as bass
import concourse.mybir as mybir
from concourse.bass_utils import run_bass_kernel_spmd

F32 = mybir.dt.float32
BF16 = mybir.dt.bfloat16
I32 = mybir.dt.int32
AF = mybir.ActivationFunctionType
ALU = mybir.AluOpType
AX = mybir.AxisListType

T = 2048
D = 4096
DFF = 11008
NH = 32
NT = T // 128
NDC = D // 128
NFC = DFF // 128
NQ = T // 512
C_SCALE = float(1.0 / np.sqrt(128.0))
EXP_BIAS = -12.0
LN_EPS = 1e-5
INV_D = 1.0 / D
PARTS = [(0, 29), (29, 29), (58, 28)]


def seg_list(qi):
    kend = (qi + 1) * 128
    if kend <= 1024:
        return [(0, kend)]
    return [(0, 1024), (1024, kend - 1024)]


def n_tgroups(qi):
    return -(-(qi + 1) // 4)


SEGS = []
SI_OF = {}
LAST_SI = {}
for _qi in range(NT):
    _sl = seg_list(_qi)
    for _j, (_s, _l) in enumerate(_sl):
        SI_OF[(_qi, _j)] = len(SEGS)
        SEGS.append((_qi, _j, _s, _l))
        LAST_SI[_qi] = len(SEGS) - 1
NSEG = len(SEGS)
TGC = [0]
for _qi in range(NT):
    TGC.append(TGC[-1] + n_tgroups(_qi))
NTG = TGC[-1]
NEV = NTG // 2
CMB_CUM = [0]
for _qi in range(NT):
    CMB_CUM.append(CMB_CUM[-1] + (1 if _qi < 8 else 2))


def dev_F(p, dm):
    if p == 0:
        return dm + 1
    if p == 1:
        return 32 + dm + 1
    return 64 + 2 * (dm + 1)


class Sem:
    def __init__(self, nc, e, name):
        self.h = e(nc.semaphore(name))


def w16(eng, sems, k):
    eng.wait_ge(sems[k % 2].h, 16 * (k // 2 + 1))


def wall16(eng, sems, n):
    eng.wait_ge(sems[0].h, 16 * ((n + 1) // 2))
    if n // 2:
        eng.wait_ge(sems[1].h, 16 * (n // 2))


def i16(ins, sems, k):
    ins.then_inc(sems[k % 2].h, 16)


def build_nc():
    nc = bass.Bass("TRN2", target_bir_lowering=False)

    x_d = nc.dram_tensor("x", [T, D], F32, kind="ExternalInput").ap()
    wq_d = nc.dram_tensor("wq_p", [NH, 128, D], F32, kind="ExternalInput").ap()
    wk_d = nc.dram_tensor("wk_p", [128, D], F32, kind="ExternalInput").ap()
    wv_d = nc.dram_tensor("wv_p", [128, D], F32, kind="ExternalInput").ap()
    wo_d = nc.dram_tensor("wo_p", [NDC, 128, D], F32, kind="ExternalInput").ap()
    gw_d = nc.dram_tensor("gw_p", [NFC, 128, D], I32, kind="ExternalInput").ap()
    uw_d = nc.dram_tensor("uw_p", [NFC, 128, D], I32, kind="ExternalInput").ap()
    dw_d = nc.dram_tensor("dw_p", [NDC, 128, DFF], I32, kind="ExternalInput").ap()
    gs_d = nc.dram_tensor("gs_p", [128, NFC], F32, kind="ExternalInput").ap()
    us_d = nc.dram_tensor("us_p", [128, NFC], F32, kind="ExternalInput").ap()
    ds_d = nc.dram_tensor("ds_p", [128, NDC], F32, kind="ExternalInput").ap()
    out_d = nc.dram_tensor("out", [T, D], F32, kind="ExternalOutput").ap()

    xn_dram = nc.dram_tensor("xn_dram", [T, D], BF16, kind="Internal").ap()
    oT_dram = nc.dram_tensor("oT_dram", [NH, 128, T], BF16, kind="Internal").ap()
    h_dram = nc.dram_tensor("h_dram", [DFF, T], BF16, kind="Internal").ap()
    attnT_dram = nc.dram_tensor("attnT_dram", [D, T], BF16, kind="Internal").ap()
    downT_dram = nc.dram_tensor("downT_dram", [D, T], BF16, kind="Internal").ap()
    dpart_dram = nc.dram_tensor("dpart_dram", [D, T], F32, kind="Internal").ap()

    with ExitStack() as ctx:
        e = ctx.enter_context

        cst = e(nc.sbuf_tensor("cst", [128, 8], F32))
        maskM = e(nc.sbuf_tensor("maskM", [128, 896], F32))
        ident = e(nc.sbuf_tensor("ident", [128, 128], BF16))
        ones_c = e(nc.sbuf_tensor("ones_c", [128, 1], BF16))
        ones_r = e(nc.sbuf_tensor("ones_r", [1, 128], F32))
        inv_sb = e(nc.sbuf_tensor("inv_sb", [1, 512], F32))
        invb = e(nc.sbuf_tensor("invb", [128, 512], BF16))
        gs_s = e(nc.sbuf_tensor("gs_s", [128, NFC], F32))
        us_s = e(nc.sbuf_tensor("us_s", [128, NFC], F32))
        ds_s = e(nc.sbuf_tensor("ds_s", [128, NDC], F32))
        kT = e(nc.sbuf_tensor("kT", [128, T], BF16))
        vtok = e(nc.sbuf_tensor("vtok", [128, T], BF16))
        sums0 = e(nc.sbuf_tensor("sums0", [128, 64], F32))
        sums1 = e(nc.sbuf_tensor("sums1", [128, 64], F32))
        stat = e(nc.sbuf_tensor("stat", [128, 16], F32))

        S = lambda name: Sem(nc, e, name)
        s_dsc = S("s_dsc")
        s_dxl = [S("s_dxl0"), S("s_dxl1")]
        s_dxn = [S("s_dxn0"), S("s_dxn1")]
        s_dxt = S("s_dxt")
        s_dwkv = S("s_dwkv")
        s_dwq = [S("s_dwq0"), S("s_dwq1")]
        s_dgw = [S("s_dgw0"), S("s_dgw1")]
        s_duw = [S("s_duw0"), S("s_duw1")]
        s_dwo = [S("s_dwo0"), S("s_dwo1")]
        s_ddw = [S("s_ddw0"), S("s_ddw1")]
        s_doS = [S("s_doS0"), S("s_doS1")]
        s_dolS = S("s_dolS")
        s_dolC = S("s_dolC")
        s_dhS = [S("s_dhS0"), S("s_dhS1")]
        s_dhL = S("s_dhL")
        s_daS = [S("s_daS0"), S("s_daS1")]
        s_dpS = [S("s_dpS0"), S("s_dpS1")]
        s_dpL = [S("s_dpL0"), S("s_dpL1")]
        s_dnS = [S("s_dnS0"), S("s_dnS1")]
        s_d6 = [S("s_d60"), S("s_d61")]
        s_dx6 = [S("s_dx60"), S("s_dx61")]
        s_dout = [S("s_dout0"), S("s_dout1")]
        s_pool = S("s_pool")
        s_lnV = S("s_lnV")
        s_lnA = S("s_lnA")
        s_kv = S("s_kv")
        s_kve = S("s_kve")
        s_vtr = S("s_vtr")
        s_vte = S("s_vte")
        s_q = S("s_q")
        s_qe = S("s_qe")
        s_st = S("s_st")
        s_exp = S("s_exp")
        s_msk = S("s_msk")
        s_on = S("s_on")
        s_iv = S("s_iv")
        s_ib = S("s_ib")
        s_ive = S("s_ive")
        s_pv = S("s_pv")
        s_pso = S("s_pso")
        s_gmm = S("s_gmm")
        s_umm = S("s_umm")
        s_silu = S("s_silu")
        s_hm = S("s_hm")
        s_omm = S("s_omm")
        s_oev = S("s_oev")
        s_dmm = S("s_dmm")
        s_dev = S("s_dev")
        s_sum = S("s_sum")

        block = e(nc.Block())

        consts = [EXP_BIAS, C_SCALE, 0.0, 1.0, LN_EPS, -0.5]
        for i, cv in enumerate(consts):
            nc.const_aps.register(F32, cv, cst[:, i:i + 1])

        p1_cms = [nc.sbuf_tensor("xt0", [128, D], F32, side="right"),
                  nc.sbuf_tensor("xt1", [128, D], F32, side="right"),
                  nc.sbuf_tensor("xnb0", [128, D], BF16, side="right"),
                  nc.sbuf_tensor("xnb1", [128, D], BF16, side="right"),
                  nc.sbuf_tensor("sqscr", [128, D], BF16, side="right")]
        xt0, xt1, xnb0, xnb1, sqscr = [cm.__enter__() for cm in p1_cms]
        xt = [xt0, xt1]
        xnb = [xnb0, xnb1]

        @block.gpsimd
        def _(g):
            for i, cv in enumerate(consts):
                g.memset(cst[:, i:i + 1], cv)
            g.memset(maskM[:], 0.0).then_inc(s_pool.h, 1)
            g.memset(ident[:], 1.0).then_inc(s_pool.h, 1)
            g.memset(ones_c[:], 1.0)
            g.memset(ones_r[:], 1.0)
            g.wait_ge(s_pool.h, 2)
            g.affine_select(out=maskM[:], in_=maskM[:], pattern=[[1, 896]],
                            compare_op=ALU.is_ge, fill=-1e30, base=-384,
                            channel_multiplier=-1).then_inc(s_pool.h, 1)
            g.wait_ge(s_pool.h, 3)
            g.affine_select(out=ident[:], in_=ident[:], pattern=[[1, 128]],
                            compare_op=ALU.is_equal, fill=0.0, base=0,
                            channel_multiplier=-1).then_inc(s_pool.h, 1)
            for src, dst in ((gs_d, gs_s), (us_d, us_s), (ds_d, ds_s)):
                g.dma_start(out=dst[:], in_=src).then_inc(s_dsc.h, 16)

        sc1 = nc.enter_named_scope('p1_ln', False)

        @block.sync
        def _(s):
            def xn_store(j):
                s.wait_ge(s_lnV.h, 5 * j + 5)
                s.dma_start(out=xn_dram[j * 128:(j + 1) * 128, :],
                            in_=xnb[j % 2][:]).then_inc(s_dxn[j % 2].h, 16)
            for i in range(NT):
                if i >= 2:
                    xn_store(i - 2)
                s.dma_start(out=xt[i % 2][:],
                            in_=x_d[i * 128:(i + 1) * 128, :]).then_inc(s_dxl[i % 2].h, 16)
            xn_store(NT - 2)
            xn_store(NT - 1)

        @block.vector
        def _(v):
            v.wait_ge(s_pool.h, 4)
            for i in range(NT):
                c0 = (i % 2) * 8
                w16(v, s_dxl, i)
                v.reduce_sum(stat[:, c0:c0 + 1], xt[i % 2][:], axis=AX.X).then_inc(s_lnV.h, 1)
                v.wait_ge(s_lnV.h, 5 * i + 1)
                v.tensor_scalar(out=stat[:, c0 + 1:c0 + 2], in0=stat[:, c0:c0 + 1],
                                scalar1=-INV_D, scalar2=None, op0=ALU.mult).then_inc(s_lnV.h, 1)
                v.wait_ge(s_lnV.h, 5 * i + 2)
                v.tensor_tensor(out=stat[:, c0 + 2:c0 + 3], in0=stat[:, c0 + 1:c0 + 2],
                                in1=stat[:, c0 + 1:c0 + 2], op=ALU.mult).then_inc(s_lnV.h, 1)
                v.wait_ge(s_lnA.h, 3 * i + 1)
                v.wait_ge(s_lnV.h, 5 * i + 3)
                v.tensor_scalar(out=stat[:, c0 + 4:c0 + 5], in0=stat[:, c0 + 3:c0 + 4],
                                scalar1=INV_D, scalar2=stat[:, c0 + 2:c0 + 3],
                                op0=ALU.mult, op1=ALU.subtract).then_inc(s_lnV.h, 1)
                v.wait_ge(s_lnA.h, 3 * i + 3)
                if i >= 2:
                    w16(v, s_dxn, i - 2)
                v.tensor_scalar(out=xnb[i % 2][:], in0=xt[i % 2][:],
                                scalar1=stat[:, c0 + 1:c0 + 2],
                                scalar2=stat[:, c0 + 6:c0 + 7],
                                op0=ALU.add, op1=ALU.mult).then_inc(s_lnV.h, 1)

        @block.scalar
        def _(sc):
            sc.wait_ge(s_pool.h, 4)
            for i in range(NT):
                c0 = (i % 2) * 8
                w16(sc, s_dxl, i)
                if i >= 2:
                    sc.wait_ge(s_lnV.h, 5 * (i - 2) + 4)
                sc.activation(out=sqscr[:], in_=xt[i % 2][:], func=AF.Square,
                              bias=0.0, scale=1.0,
                              accum_out=stat[:, c0 + 3:c0 + 4]).then_inc(s_lnA.h, 1)
                sc.wait_ge(s_lnV.h, 5 * i + 4)
                sc.activation(out=stat[:, c0 + 5:c0 + 6], in_=stat[:, c0 + 4:c0 + 5],
                              func=AF.Ln, bias=LN_EPS, scale=1.0).then_inc(s_lnA.h, 1)
                sc.wait_ge(s_lnA.h, 3 * i + 2)
                sc.activation(out=stat[:, c0 + 6:c0 + 7], in_=stat[:, c0 + 5:c0 + 6],
                              func=AF.Exp, bias=0.0, scale=-0.5).then_inc(s_lnA.h, 1)

        nc.leave_named_scope('p1_ln', sc1[0], False)

        for cm in reversed(p1_cms):
            cm.__exit__(None, None, None)
        xnT_cm = nc.sbuf_tensor("xnT", [128, NDC * T], BF16)
        xnT = xnT_cm.__enter__()
        p3_cms = [nc.sbuf_tensor("wq_b0", [128, D], BF16, side="right"),
                  nc.sbuf_tensor("wq_b1", [128, D], BF16, side="right"),
                  nc.sbuf_tensor("qT_h", [128, T], BF16, side="right"),
                  nc.sbuf_tensor("PT", [128, NT * 512], BF16, side="right"),
                  nc.sbuf_tensor("oT_strip0", [128, T], BF16, side="right"),
                  nc.sbuf_tensor("oT_strip1", [128, T], BF16, side="right")]
        wq_b0, wq_b1, qT_h, PT, oT_s0, oT_s1 = [cm.__enter__() for cm in p3_cms]
        wq_b = [wq_b0, wq_b1]
        oT_strip = [oT_s0, oT_s1]
        vT_tmp = qT_h
        wkv_b = PT
        ps_big = e(nc.psum_tensor("ps_big", [128, 2048], F32))
        pt_cms = [nc.psum_tensor("ps_t0", [128, 1024], BF16),
                  nc.psum_tensor("ps_t1", [128, 1024], BF16)]
        ps_t0, ps_t1 = [cm.__enter__() for cm in pt_cms]

        @block.sync
        def _(s):
            wall16(s, s_dxn, NT)
            for dc in range(NDC):
                s.dma_start_transpose(xnT[:, dc * T:(dc + 1) * T],
                                      xn_dram[:, dc * 128:(dc + 1) * 128]).then_inc(s_dxt.h, 16)

        @block.gpsimd
        def _(g):
            wall16(g, s_dxl, NT)
            wall16(g, s_dxn, NT)
            g.wait_ge(s_lnV.h, 5 * NT)
            g.wait_ge(s_lnA.h, 3 * NT)
            g.dma_start(out=wkv_b[:, 0:D], in_=wk_d).then_inc(s_dwkv.h, 16)
            g.dma_start(out=wkv_b[:, D:2 * D], in_=wv_d).then_inc(s_dwkv.h, 16)
            for h in range(NH):
                if h >= 2:
                    g.wait_ge(s_q.h, h - 1)
                g.dma_start(out=wq_b[h % 2][:], in_=wq_d[h]).then_inc(s_dwq[h % 2].h, 16)

        sc2 = nc.enter_named_scope('p2_kv', False)

        @block.tensor
        def _(t):
            t.wait_ge(s_pool.h, 4)
            t.wait_ge(s_dwkv.h, 32)
            t.wait_ge(s_dxt.h, 16 * NDC)
            for which in range(2):
                if which == 1:
                    t.wait_ge(s_kve.h, 1)
                for dc in range(NDC):
                    for m in range(4):
                        ins = t.matmul(
                            ps_big[:, m * 512:(m + 1) * 512],
                            wkv_b[:, which * D + dc * 128:which * D + (dc + 1) * 128],
                            xnT[:, dc * T + m * 512:dc * T + (m + 1) * 512],
                            start=(dc == 0), stop=(dc == NDC - 1))
                ins.then_inc(s_kv.h, 1)
            for kb in range(NT):
                if kb == 0:
                    t.wait_ge(s_kve.h, 2)
                if kb >= 2:
                    t.wait_ge(s_vte.h, kb - 1)
                pst = ps_t0 if kb % 2 == 0 else ps_t1
                t.transpose(pst[:, 0:128], vT_tmp[:, kb * 128:(kb + 1) * 128],
                            ident[:]).then_inc(s_vtr.h, 1)

        @block.vector
        def _(v):
            v.wait_ge(s_pool.h, 4)
            for which in range(2):
                v.wait_ge(s_kv.h, which + 1)
                dst = kT if which == 0 else vT_tmp
                v.tensor_copy(dst[:], ps_big[:]).then_inc(s_kve.h, 1)
            for kb in range(NT):
                v.wait_ge(s_vtr.h, kb + 1)
                pst = ps_t0 if kb % 2 == 0 else ps_t1
                v.tensor_copy(vtok[:, kb * 128:(kb + 1) * 128],
                              pst[:, 0:128]).then_inc(s_vte.h, 1)

        nc.leave_named_scope('p2_kv', sc2[0], False)

        for cm in reversed(pt_cms):
            cm.__exit__(None, None, None)
        p3ps_cms = [nc.psum_tensor("psS", [128, 512], F32),
                    nc.psum_tensor("psO0", [128, 512], F32),
                    nc.psum_tensor("psO1", [128, 512], F32)]
        psS, psO0, psO1 = [cm.__enter__() for cm in p3ps_cms]
        psO = [psO0, psO1]

        sc3 = nc.enter_named_scope('p3_attn', False)

        CH0 = [0]
        for _qq in range(NQ):
            CH0.append(CH0[-1] + 4 * _qq + 4)
        NCH = CH0[-1]

        @block.tensor
        def _(t):
            t.wait_ge(s_pool.h, 4)
            for h in range(NH):
                w16(t, s_dwq, h)
                if h == 0:
                    t.wait_ge(s_vte.h, NT)
                else:
                    t.wait_ge(s_exp.h, h * NCH)
                    t.wait_ge(s_ive.h, h * NQ)
                for dc in range(NDC):
                    for m in range(4):
                        ins = t.matmul(
                            ps_big[:, m * 512:(m + 1) * 512],
                            wq_b[h % 2][:, dc * 128:(dc + 1) * 128],
                            xnT[:, dc * T + m * 512:dc * T + (m + 1) * 512],
                            start=(dc == 0), stop=(dc == NDC - 1))
                ins.then_inc(s_q.h, 1)

                def ones_pv(cp, h=h):
                    qq = next(q for q in range(NQ) if CH0[q] <= cp < CH0[q + 1])
                    j = cp - CH0[qq]
                    last = (j == 4 * qq + 3)
                    t.wait_ge(s_exp.h, h * NCH + cp + 1)
                    if j == 0:
                        t.wait_ge(s_iv.h, h * NQ + qq)
                        t.wait_ge(s_pso.h, h * NQ + max(0, qq - 1))
                    q0 = max(0, j * 128 - qq * 512)
                    ins = t.matmul(psS[0:1, q0:512], ones_c[:],
                                   PT[:, j * 512 + q0:(j + 1) * 512],
                                   start=(j == 0), stop=last)
                    if last:
                        ins.then_inc(s_on.h, 1)
                    ins = t.matmul(psO[qq % 2][:, q0:512],
                                   vtok[:, j * 128:(j + 1) * 128],
                                   PT[:, j * 512 + q0:(j + 1) * 512],
                                   start=(j == 0), stop=last)
                    if last:
                        ins.then_inc(s_pv.h, 1)

                for qq in range(NQ):
                    for j in range(4 * qq + 4):
                        c = CH0[qq] + j
                        if c >= 3:
                            t.wait_ge(s_exp.h, h * NCH + c - 2)
                        else:
                            t.wait_ge(s_qe.h, h + 1)
                        ins = t.matmul(
                            ps_big[:, (c % 3) * 512:(c % 3 + 1) * 512],
                            kT[:, j * 128:(j + 1) * 128],
                            qT_h[:, qq * 512:(qq + 1) * 512],
                            start=True, stop=True)
                        ins.then_inc(s_st.h, 1)
                        if c >= 2:
                            ones_pv(c - 2)
                        if qq >= 1 and j == 1:
                            t.wait_ge(s_iv.h, h * NQ + qq)
                            t.wait_ge(s_ive.h, h * NQ + qq - 1)
                            t.matmul(ps_big[:, 1536:2048], ones_r[:],
                                     inv_sb[:], start=True,
                                     stop=True).then_inc(s_ib.h, 1)
                ones_pv(NCH - 2)
                ones_pv(NCH - 1)
                t.wait_ge(s_iv.h, h * NQ + NQ)
                t.wait_ge(s_ive.h, h * NQ + NQ - 1)
                t.matmul(ps_big[:, 1536:2048], ones_r[:],
                         inv_sb[:], start=True, stop=True).then_inc(s_ib.h, 1)

        @block.scalar
        def _(sc):
            sc.wait_ge(s_pool.h, 4)
            for h in range(NH):
                sc.wait_ge(s_q.h, h + 1)
                sc.activation(out=qT_h[:], in_=ps_big[:], func=AF.Copy,
                              bias=0.0, scale=1.0).then_inc(s_qe.h, 1)
                for qq in range(NQ):
                    for j in range(4 * qq + 4):
                        c = CH0[qq] + j
                        sc.wait_ge(s_st.h, h * NCH + c + 1)
                        if j >= 4 * qq:
                            k = j - 4 * qq
                            sc.wait_ge(s_msk.h, h * 16 + qq * 4 + k + 1)
                        sc.wait_ge(s_pv.h, h * NQ + qq)
                        sc.activation(
                            out=PT[:, j * 512:(j + 1) * 512],
                            in_=ps_big[:, (c % 3) * 512:(c % 3 + 1) * 512],
                            func=AF.Exp, bias=EXP_BIAS,
                            scale=C_SCALE).then_inc(s_exp.h, 1)

        @block.vector
        def _(v):
            v.wait_ge(s_pool.h, 4)
            for h in range(NH):
                for qq in range(NQ):
                    if qq >= 1:
                        qp = qq - 1
                        v.wait_ge(s_on.h, h * NQ + qp + 1)
                        v.wait_ge(s_ib.h, h * NQ + qp)
                        v.reciprocal(inv_sb[:], psS[0:1, :]).then_inc(s_iv.h, 1)
                        v.wait_ge(s_ib.h, h * NQ + qp + 1)
                        v.tensor_copy(invb[:], ps_big[:, 1536:2048]).then_inc(s_ive.h, 1)
                        v.wait_ge(s_pv.h, h * NQ + qp + 1)
                        v.wait_ge(s_ive.h, h * NQ + qp + 1)
                        if h >= 2 and qp == 0:
                            w16(v, s_doS, h - 2)
                        v.tensor_tensor(out=oT_strip[h % 2][:, qp * 512:(qp + 1) * 512],
                                        in0=psO[qp % 2][:], in1=invb[:],
                                        op=ALU.mult).then_inc(s_pso.h, 1)
                    for k in range(4):
                        c = CH0[qq] + 4 * qq + k
                        v.wait_ge(s_st.h, h * NCH + c + 1)
                        sl = ps_big[:, (c % 3) * 512:(c % 3 + 1) * 512]
                        v.tensor_tensor(
                            out=sl, in0=sl,
                            in1=maskM[:, 384 - 128 * k:896 - 128 * k],
                            op=ALU.add).then_inc(s_msk.h, 1)
                qp = NQ - 1
                v.wait_ge(s_on.h, h * NQ + qp + 1)
                v.wait_ge(s_ib.h, h * NQ + qp)
                v.reciprocal(inv_sb[:], psS[0:1, :]).then_inc(s_iv.h, 1)
                v.wait_ge(s_ib.h, h * NQ + qp + 1)
                v.tensor_copy(invb[:], ps_big[:, 1536:2048]).then_inc(s_ive.h, 1)
                v.wait_ge(s_pv.h, h * NQ + qp + 1)
                v.wait_ge(s_ive.h, h * NQ + qp + 1)
                v.tensor_tensor(out=oT_strip[h % 2][:, qp * 512:(qp + 1) * 512],
                                in0=psO[qp % 2][:], in1=invb[:],
                                op=ALU.mult).then_inc(s_pso.h, 1)

        @block.sync
        def _(s):
            for h in range(NH):
                s.wait_ge(s_pso.h, h * NQ + NQ)
                s.dma_start(out=oT_dram[h], in_=oT_strip[h % 2][:]).then_inc(s_doS[h % 2].h, 16)

        nc.leave_named_scope('p3_attn', sc3[0], False)

        for cm in reversed(p3_cms):
            cm.__exit__(None, None, None)
        p35_cms = [nc.sbuf_tensor("Wg_b0", [128, D], BF16, side="right"),
                   nc.sbuf_tensor("Wg_b1", [128, D], BF16, side="right"),
                   nc.sbuf_tensor("Wu_b0", [128, D], BF16, side="right"),
                   nc.sbuf_tensor("Wu_b1", [128, D], BF16, side="right"),
                   nc.sbuf_tensor("silu_g", [128, T], BF16, side="right"),
                   nc.sbuf_tensor("h_buf0", [128, T], BF16, side="right"),
                   nc.sbuf_tensor("h_buf1", [128, T], BF16, side="right")]
        Wg0, Wg1, Wu0, Wu1, silu_g, hb0, hb1 = [cm.__enter__() for cm in p35_cms]
        Wg_b = [Wg0, Wg1]
        Wu_b = [Wu0, Wu1]
        h_buf = [hb0, hb1]
        for cm in reversed(p3ps_cms):
            cm.__exit__(None, None, None)
        ps_up_cm = nc.psum_tensor("ps_up", [128, 2048], F32)
        ps_up = ps_up_cm.__enter__()

        @block.gpsimd
        def _(g):
            g.wait_ge(s_exp.h, NH * 40)
            g.wait_ge(s_iv.h, NH * NQ)
            g.wait_ge(s_ive.h, NH * NQ)
            g.wait_ge(s_pso.h, NH * NQ)
            wall16(g, s_doS, NH)
            wall16(g, s_dwq, NH)
            for fc in range(NFC):
                if fc >= 2:
                    g.wait_ge(s_gmm.h, fc - 1)
                g.dma_start(out=Wg_b[fc % 2][:], in_=gw_d[fc]).then_inc(s_dgw[fc % 2].h, 16)
                if fc >= 2:
                    g.wait_ge(s_umm.h, fc - 1)
                g.dma_start(out=Wu_b[fc % 2][:], in_=uw_d[fc]).then_inc(s_duw[fc % 2].h, 16)

        sc35 = nc.enter_named_scope('p35_gateup', False)

        @block.tensor
        def _(t):
            for fc in range(NFC):
                w16(t, s_dgw, fc)
                if fc == 0:
                    t.wait_ge(s_exp.h, NH * 40)
                    t.wait_ge(s_msk.h, NH * 16)
                    t.wait_ge(s_iv.h, NH * NQ)
                    t.wait_ge(s_ive.h, NH * NQ)
                    t.wait_ge(s_pso.h, NH * NQ)
                else:
                    t.wait_ge(s_silu.h, fc)
                for dc in range(NDC):
                    for m in range(4):
                        ins = t.matmul(
                            ps_big[:, m * 512:(m + 1) * 512],
                            Wg_b[fc % 2][:, dc * 128:(dc + 1) * 128],
                            xnT[:, dc * T + m * 512:dc * T + (m + 1) * 512],
                            start=(dc == 0), stop=(dc == NDC - 1))
                ins.then_inc(s_gmm.h, 1)
                w16(t, s_duw, fc)
                if fc >= 1:
                    t.wait_ge(s_hm.h, fc)
                for dc in range(NDC):
                    for m in range(4):
                        ins = t.matmul(
                            ps_up[:, m * 512:(m + 1) * 512],
                            Wu_b[fc % 2][:, dc * 128:(dc + 1) * 128],
                            xnT[:, dc * T + m * 512:dc * T + (m + 1) * 512],
                            start=(dc == 0), stop=(dc == NDC - 1))
                ins.then_inc(s_umm.h, 1)

        @block.scalar
        def _(sc):
            sc.wait_ge(s_dsc.h, 48)
            for fc in range(NFC):
                sc.wait_ge(s_gmm.h, fc + 1)
                if fc >= 1:
                    sc.wait_ge(s_hm.h, fc)
                sc.activation(out=silu_g[:], in_=ps_big[:], func=AF.Silu,
                              bias=0.0, scale=gs_s[:, fc:fc + 1]).then_inc(s_silu.h, 1)

        @block.vector
        def _(v):
            for fc in range(NFC):
                v.wait_ge(s_umm.h, fc + 1)
                v.wait_ge(s_silu.h, fc + 1)
                if fc >= 2:
                    w16(v, s_dhS, fc - 2)
                v.scalar_tensor_tensor(
                    out=h_buf[fc % 2][:], in0=ps_up[:],
                    scalar=us_s[:, fc:fc + 1], in1=silu_g[:],
                    op0=ALU.mult, op1=ALU.mult).then_inc(s_hm.h, 1)

        @block.sync
        def _(s):
            for fc in range(NFC):
                s.wait_ge(s_hm.h, fc + 1)
                s.dma_start(out=h_dram[fc * 128:(fc + 1) * 128, :],
                            in_=h_buf[fc % 2][:]).then_inc(s_dhS[fc % 2].h, 16)

        nc.leave_named_scope('p35_gateup', sc35[0], False)

        for cm in reversed(p35_cms):
            cm.__exit__(None, None, None)
        xnT_cm.__exit__(None, None, None)
        p4_cms = [nc.sbuf_tensor("oT_all", [128, NH * T], BF16, side="right"),
                  nc.sbuf_tensor("wo_b0", [128, D], BF16, side="right"),
                  nc.sbuf_tensor("wo_b1", [128, D], BF16, side="right"),
                  nc.sbuf_tensor("at_b0", [128, T], BF16, side="right"),
                  nc.sbuf_tensor("at_b1", [128, T], BF16, side="right")]
        oT_all, wo_b0, wo_b1, at_b0, at_b1 = [cm.__enter__() for cm in p4_cms]
        wo_b = [wo_b0, wo_b1]
        at_b = [at_b0, at_b1]

        @block.gpsimd
        def _(g):
            g.wait_ge(s_umm.h, NFC)
            g.wait_ge(s_hm.h, NFC)
            wall16(g, s_dhS, NFC)
            wall16(g, s_dgw, NFC)
            wall16(g, s_duw, NFC)
            for do in range(NDC):
                if do >= 2:
                    g.wait_ge(s_omm.h, do - 1)
                g.dma_start(out=wo_b[do % 2][:], in_=wo_d[do]).then_inc(s_dwo[do % 2].h, 16)

        sc4 = nc.enter_named_scope('p4_oproj', False)

        @block.sync
        def _(s):
            s.wait_ge(s_umm.h, NFC)
            s.wait_ge(s_hm.h, NFC)
            wall16(s, s_dhS, NFC)
            wall16(s, s_dgw, NFC)
            wall16(s, s_duw, NFC)
            for di in range(0, NH, 2):
                s.dma_start(out=oT_all[:, di * T:(di + 1) * T],
                            in_=oT_dram[di]).then_inc(s_dolS.h, 16)

        @block.scalar
        def _(sc):
            sc.wait_ge(s_umm.h, NFC)
            sc.wait_ge(s_hm.h, NFC)
            wall16(sc, s_dhS, NFC)
            wall16(sc, s_dgw, NFC)
            wall16(sc, s_duw, NFC)
            for di in range(1, NH, 2):
                sc.dma_start(out=oT_all[:, di * T:(di + 1) * T],
                             in_=oT_dram[di]).then_inc(s_dolC.h, 16)

        @block.tensor
        def _(t):
            for do in range(NDC):
                psA = ps_big if do % 2 == 0 else ps_up
                w16(t, s_dwo, do)
                if do == 0:
                    t.wait_ge(s_silu.h, NFC)
                    t.wait_ge(s_hm.h, NFC)
                    t.wait_ge(s_dolS.h, 16 * 16)
                    t.wait_ge(s_dolC.h, 16 * 16)
                if do >= 2:
                    t.wait_ge(s_oev.h, do - 1)
                for di in range(NH):
                    for m in range(4):
                        ins = t.matmul(
                            psA[:, m * 512:(m + 1) * 512],
                            wo_b[do % 2][:, di * 128:(di + 1) * 128],
                            oT_all[:, di * T + m * 512:di * T + (m + 1) * 512],
                            start=(di == 0), stop=(di == NH - 1))
                ins.then_inc(s_omm.h, 1)

        @block.vector
        def _(v):
            for do in range(NDC):
                psA = ps_big if do % 2 == 0 else ps_up
                v.wait_ge(s_omm.h, do + 1)
                if do >= 2:
                    w16(v, s_daS, do - 2)
                v.tensor_copy(at_b[do % 2][:], psA[:]).then_inc(s_oev.h, 1)

        @block.sync
        def _(s):
            for do in range(NDC):
                s.wait_ge(s_oev.h, do + 1)
                s.dma_start(out=attnT_dram[do * 128:(do + 1) * 128, :],
                            in_=at_b[do % 2][:]).then_inc(s_daS[do % 2].h, 16)

        nc.leave_named_scope('p4_oproj', sc4[0], False)

        for cm in reversed(p4_cms):
            cm.__exit__(None, None, None)
        NPMAX = max(np_ for _f, np_ in PARTS)
        p5_cms = [nc.sbuf_tensor("h_part", [128, NPMAX * T], BF16, side="right"),
                  nc.sbuf_tensor("Wd_b0", [128, NPMAX * 128], BF16, side="right"),
                  nc.sbuf_tensor("Wd_b1", [128, NPMAX * 128], BF16, side="right"),
                  nc.sbuf_tensor("dp_in0", [128, T], F32, side="right"),
                  nc.sbuf_tensor("dp_in1", [128, T], F32, side="right"),
                  nc.sbuf_tensor("dp_out0", [128, T], F32, side="right"),
                  nc.sbuf_tensor("dp_out1", [128, T], F32, side="right"),
                  nc.sbuf_tensor("dn_b0", [128, T], BF16, side="right"),
                  nc.sbuf_tensor("dn_b1", [128, T], BF16, side="right")]
        (h_part, Wd_b0, Wd_b1, dp_in0, dp_in1,
         dp_out0, dp_out1, dn_b0, dn_b1) = [cm.__enter__() for cm in p5_cms]
        Wd_b = [Wd_b0, Wd_b1]
        dp_in = [dp_in0, dp_in1]
        dp_out = [dp_out0, dp_out1]
        dn_b = [dn_b0, dn_b1]

        hl_base = {}
        _c = 0
        for p, (fc0, npart) in enumerate(PARTS):
            hl_base[p] = _c
            _c += npart

        @block.gpsimd
        def _(g):
            g.wait_ge(s_omm.h, NDC)
            g.wait_ge(s_oev.h, NDC)
            wall16(g, s_daS, NDC)
            wall16(g, s_dwo, NDC)
            g.wait_ge(s_dolS.h, 16 * 16)
            g.wait_ge(s_dolC.h, 16 * 16)
            for p, (fc0, npart) in enumerate(PARTS):
                for dm in range(NDC):
                    if p * 32 + dm >= 2:
                        g.wait_ge(s_dmm.h, p * 32 + dm - 1)
                    g.dma_start(out=Wd_b[dm % 2][:, 0:npart * 128],
                                in_=dw_d[dm][:, fc0 * 128:(fc0 + npart) * 128]).then_inc(s_ddw[(p * 32 + dm) % 2].h, 16)

        sc5 = nc.enter_named_scope('p5_down', False)

        @block.sync
        def _(s):
            for p, (fc0, npart) in enumerate(PARTS):
                if p == 0:
                    s.wait_ge(s_omm.h, NDC)
                    s.wait_ge(s_oev.h, NDC)
                    wall16(s, s_daS, NDC)
                    wall16(s, s_dhS, NFC)
                    wall16(s, s_dwo, NDC)
                    s.wait_ge(s_dolS.h, 16 * 16)
                    s.wait_ge(s_dolC.h, 16 * 16)
                else:
                    s.wait_ge(s_dmm.h, p * 32)
                for j in range(npart):
                    fc = fc0 + j
                    s.dma_start(out=h_part[:, j * T:(j + 1) * T],
                                in_=h_dram[fc * 128:(fc + 1) * 128, :]).then_inc(s_dhL.h, 16)
                if p < 2:
                    for dm in range(NDC):
                        s.wait_ge(s_dev.h, dev_F(p, dm))
                        s.dma_start(out=dpart_dram[dm * 128:(dm + 1) * 128, :],
                                    in_=dp_out[dm % 2][:]).then_inc(s_dpS[(p * 32 + dm) % 2].h, 16)
            for dm in range(NDC):
                s.wait_ge(s_dev.h, dev_F(2, dm))
                s.dma_start(out=downT_dram[dm * 128:(dm + 1) * 128, :],
                            in_=dn_b[dm % 2][:]).then_inc(s_dnS[dm % 2].h, 16)

        @block.tensor
        def _(t):
            for p, (fc0, npart) in enumerate(PARTS):
                for dm in range(NDC):
                    psA = ps_big if dm % 2 == 0 else ps_up
                    w16(t, s_ddw, p * 32 + dm)
                    if dm == 0:
                        t.wait_ge(s_dhL.h, 16 * (hl_base[p] + npart))
                    if dm >= 2:
                        t.wait_ge(s_dev.h, dev_F(p, dm - 2))
                    elif p >= 1:
                        t.wait_ge(s_dev.h, dev_F(p - 1, NDC - 2 + dm))
                    for j in range(npart):
                        for m in range(4):
                            ins = t.matmul(
                                psA[:, m * 512:(m + 1) * 512],
                                Wd_b[dm % 2][:, j * 128:(j + 1) * 128],
                                h_part[:, j * T + m * 512:j * T + (m + 1) * 512],
                                start=(j == 0), stop=(j == npart - 1))
                    ins.then_inc(s_dmm.h, 1)

        @block.scalar
        def _(sc):
            for p in (1, 2):
                for dm in range(NDC):
                    w16(sc, s_dpS, (p - 1) * 32 + dm)
                    if dm >= 2:
                        sc.wait_ge(s_dev.h, dev_F(p, dm - 2))
                    elif p == 2:
                        sc.wait_ge(s_dev.h, dev_F(1, NDC - 2 + dm))
                    sc.dma_start(out=dp_in[dm % 2][:],
                                 in_=dpart_dram[dm * 128:(dm + 1) * 128, :]).then_inc(s_dpL[((p - 1) * 32 + dm) % 2].h, 16)

        @block.vector
        def _(v):
            for p, (fc0, npart) in enumerate(PARTS):
                for dm in range(NDC):
                    psA = ps_big if dm % 2 == 0 else ps_up
                    v.wait_ge(s_dmm.h, p * 32 + dm + 1)
                    if p == 0:
                        if dm >= 2:
                            w16(v, s_dpS, dm - 2)
                        v.tensor_copy(dp_out[dm % 2][:], psA[:]).then_inc(s_dev.h, 1)
                    elif p == 1:
                        w16(v, s_dpL, dm)
                        if dm >= 2:
                            w16(v, s_dpS, 32 + dm - 2)
                        else:
                            w16(v, s_dpS, NDC - 2 + dm)
                        v.tensor_tensor(out=dp_out[dm % 2][:], in0=psA[:],
                                        in1=dp_in[dm % 2][:], op=ALU.add).then_inc(s_dev.h, 1)
                    else:
                        w16(v, s_dpL, 32 + dm)
                        if dm >= 2:
                            w16(v, s_dnS, dm - 2)
                            wall16(v, s_dpS, 64)
                        else:
                            w16(v, s_dpS, 32 + NDC - 2 + dm)
                        v.tensor_tensor(out=dp_out[dm % 2][:], in0=psA[:],
                                        in1=dp_in[dm % 2][:], op=ALU.add).then_inc(s_dev.h, 1)
                        v.wait_ge(s_dev.h, 64 + 2 * dm + 1)
                        v.tensor_scalar(out=dn_b[dm % 2][:], in0=dp_out[dm % 2][:],
                                        scalar1=ds_s[:, dm:dm + 1], scalar2=None,
                                        op0=ALU.mult).then_inc(s_dev.h, 1)

        nc.leave_named_scope('p5_down', sc5[0], False)

        for cm in reversed(p5_cms):
            cm.__exit__(None, None, None)
        p6_cms = [nc.sbuf_tensor("x6_0", [128, D], F32, side="right"),
                  nc.sbuf_tensor("x6_1", [128, D], F32, side="right"),
                  nc.sbuf_tensor("a6_0", [128, D], BF16, side="right"),
                  nc.sbuf_tensor("a6_1", [128, D], BF16, side="right"),
                  nc.sbuf_tensor("d6_0", [128, D], BF16, side="right"),
                  nc.sbuf_tensor("d6_1", [128, D], BF16, side="right"),
                  nc.sbuf_tensor("o6_0", [128, D], F32, side="right"),
                  nc.sbuf_tensor("o6_1", [128, D], F32, side="right")]
        x6_0, x6_1, a6_0, a6_1, d6_0, d6_1, o6_0, o6_1 = [cm.__enter__() for cm in p6_cms]
        x6 = [x6_0, x6_1]
        a6 = [a6_0, a6_1]
        d6 = [d6_0, d6_1]
        o6 = [o6_0, o6_1]

        sc6 = nc.enter_named_scope('p6_final', False)

        @block.sync
        def _(s):
            s.wait_ge(s_dmm.h, 96)
            s.wait_ge(s_dev.h, dev_F(2, NDC - 1))
            wall16(s, s_dnS, NDC)
            wall16(s, s_dpS, 64)
            wall16(s, s_ddw, 96)
            s.wait_ge(s_dhL.h, 16 * 86)
            wall16(s, s_dpL, 64)
            def out_store(j):
                s.wait_ge(s_sum.h, 2 * (j + 1))
                s.dma_start(out=out_d[j * 128:(j + 1) * 128, :],
                            in_=o6[j % 2][:]).then_inc(s_dout[j % 2].h, 16)
            for i in range(NT):
                if i >= 2:
                    out_store(i - 2)
                s.dma_start_transpose(a6[i % 2][:],
                                      attnT_dram[:, i * 128:(i + 1) * 128]).then_inc(s_d6[i % 2].h, 16)
                s.dma_start_transpose(d6[i % 2][:],
                                      downT_dram[:, i * 128:(i + 1) * 128]).then_inc(s_d6[i % 2].h, 16)
            out_store(NT - 2)
            out_store(NT - 1)
            wall16(s, s_dout, NT)

        @block.scalar
        def _(sc):
            sc.wait_ge(s_dmm.h, 96)
            sc.wait_ge(s_dev.h, dev_F(2, NDC - 1))
            wall16(sc, s_dnS, NDC)
            wall16(sc, s_ddw, 96)
            sc.wait_ge(s_dhL.h, 16 * 86)
            wall16(sc, s_dpL, 64)
            for i in range(NT):
                if i >= 2:
                    sc.wait_ge(s_sum.h, 2 * (i - 1))
                sc.dma_start(out=x6[i % 2][:],
                             in_=x_d[i * 128:(i + 1) * 128, :]).then_inc(s_dx6[i % 2].h, 16)

        @block.vector
        def _(v):
            for i in range(NT):
                v.wait_ge(s_d6[i % 2].h, 32 * (i // 2 + 1))
                w16(v, s_dx6, i)
                if i >= 2:
                    w16(v, s_dout, i - 2)
                v.tensor_tensor(out=o6[i % 2][:], in0=a6[i % 2][:],
                                in1=d6[i % 2][:], op=ALU.add).then_inc(s_sum.h, 1)
                v.wait_ge(s_sum.h, 2 * i + 1)
                v.tensor_tensor(out=o6[i % 2][:], in0=o6[i % 2][:],
                                in1=x6[i % 2][:], op=ALU.add).then_inc(s_sum.h, 1)

        nc.leave_named_scope('p6_final', sc6[0], False)

        for cm in reversed(p6_cms):
            cm.__exit__(None, None, None)
        ps_up_cm.__exit__(None, None, None)

    return nc


def _host_layout(wq, wk, wv, wo, gw, uw, dw, gs, us, ds):
    wq_p = np.ascontiguousarray(
        wq.reshape(NDC, 128, NH, 128).transpose(2, 1, 0, 3).reshape(NH, 128, D))
    wo_p = np.ascontiguousarray(
        wo.reshape(NDC, 128, NDC, 128).transpose(2, 1, 0, 3).reshape(NDC, 128, D))
    wk_p = np.ascontiguousarray(
        wk.reshape(NDC, 128, 128).transpose(1, 0, 2).reshape(128, D))
    wv_p = np.ascontiguousarray(
        wv.reshape(NDC, 128, 128).transpose(1, 0, 2).reshape(128, D))
    gw_p = np.ascontiguousarray(
        gw.reshape(NFC, 128, NDC, 128).transpose(0, 3, 2, 1).reshape(NFC, 128, D))
    uw_p = np.ascontiguousarray(
        uw.reshape(NFC, 128, NDC, 128).transpose(0, 3, 2, 1).reshape(NFC, 128, D))
    dw_p = np.ascontiguousarray(
        dw.reshape(NDC, 128, NFC, 128).transpose(0, 3, 2, 1).reshape(NDC, 128, DFF))
    gs_p = np.ascontiguousarray(gs.reshape(NFC, 128).T)
    us_p = np.ascontiguousarray(us.reshape(NFC, 128).T)
    ds_p = np.ascontiguousarray(ds.reshape(NDC, 128).T)
    return wq_p, wk_p, wv_p, wo_p, gw_p, uw_p, dw_p, gs_p, us_p, ds_p


def kernel(L0_attn_x, L0_attn_wq, L0_attn_wk, L0_attn_wv, L0_attn_wo,
           L0_ffn_gate_w, L0_ffn_gate_s, L0_ffn_up_w, L0_ffn_up_s,
           L0_ffn_down_w, L0_ffn_down_s):
    x = np.ascontiguousarray(np.asarray(L0_attn_x, dtype=np.float32))
    (wq_p, wk_p, wv_p, wo_p, gw_p, uw_p, dw_p, gs_p, us_p, ds_p) = _host_layout(
        np.asarray(L0_attn_wq, np.float32), np.asarray(L0_attn_wk, np.float32),
        np.asarray(L0_attn_wv, np.float32), np.asarray(L0_attn_wo, np.float32),
        np.asarray(L0_ffn_gate_w, np.int32), np.asarray(L0_ffn_up_w, np.int32),
        np.asarray(L0_ffn_down_w, np.int32), np.asarray(L0_ffn_gate_s, np.float32),
        np.asarray(L0_ffn_up_s, np.float32), np.asarray(L0_ffn_down_s, np.float32))

    nc = build_nc()
    in_maps = []
    for c in range(8):
        in_maps.append(dict(
            x=np.ascontiguousarray(x[c]),
            wq_p=wq_p, wk_p=wk_p, wv_p=wv_p, wo_p=wo_p,
            gw_p=gw_p, uw_p=uw_p, dw_p=dw_p,
            gs_p=gs_p, us_p=us_p, ds_p=ds_p,
        ))
    res = run_bass_kernel_spmd(nc, in_maps, core_ids=list(range(8)))
    out = np.stack([res.results[c]["out"] for c in range(8)], axis=0)
    return out
